# version 07
# baseline (speedup 1.0000x reference)
import numpy as np
import ml_dtypes
import contextlib
import concourse.bass as bass
import concourse.mybir as mybir
from concourse.bass_utils import run_bass_kernel_spmd

F32 = mybir.dt.float32
BF16 = mybir.dt.bfloat16
AF = mybir.ActivationFunctionType
ALU = mybir.AluOpType
AX = mybir.AxisListType.X

D = 4096
T = 4096
QH = 32
KVH = 8
HD = 128
GQ = QH // KVH
OWN = 1024
NQT = OWN // 128
NBT = T // 128
DC = D // 128
HALF = D // 2
SCALE = float(HD) ** -0.5
EPS = 1e-5

ENGS = ("tensor", "vector", "scalar", "gpsimd", "sync")
SEMN = ("PE", "DV", "SC", "QS", "QG")


class Prog:
    def __init__(self, sems):
        self.ops = {e: [] for e in ENGS}
        self.sems = sems
        self.cnt = {k: 0 for k in sems}

    def emit(self, eng, fn):
        self.ops[eng].append(fn)

    def wait(self, eng, deps):
        deps = [(self.sems[s], v) for s, v in deps if v > 0]
        if deps:
            def f(e, deps=deps):
                for h, v in deps:
                    e.wait_ge(h, v)
            self.emit(eng, f)

    def join(self):
        s = dict(self.cnt)
        for eng in ENGS:
            self.wait(eng, [(k, s[k]) for k in SEMN])

    def flush(self, block):
        ops = self.ops
        self.ops = {e: [] for e in ENGS}
        if ops["gpsimd"]:
            @block.gpsimd
            def _(e):
                for f in ops["gpsimd"]:
                    f(e)
        if ops["tensor"]:
            @block.tensor
            def _(e):
                for f in ops["tensor"]:
                    f(e)
        if ops["vector"]:
            @block.vector
            def _(e):
                for f in ops["vector"]:
                    f(e)
        if ops["scalar"]:
            @block.scalar
            def _(e):
                for f in ops["scalar"]:
                    f(e)
        if ops["sync"]:
            @block.sync
            def _(e):
                for f in ops["sync"]:
                    f(e)


def build_program():
    nc = bass.Bass("TRN2", target_bir_lowering=False)

    xb = nc.dram_tensor("xb", [T, D], F32, kind="ExternalInput").ap()
    wq = nc.dram_tensor("wq", [D, D], F32, kind="ExternalInput").ap()
    wk = nc.dram_tensor("wk", [D, KVH * HD], F32, kind="ExternalInput").ap()
    wv = nc.dram_tensor("wv", [D, KVH * HD], F32, kind="ExternalInput").ap()
    wo = nc.dram_tensor("wo", [D, D], F32, kind="ExternalInput").ap()
    fw = nc.dram_tensor("fw", [D, D], F32, kind="ExternalInput").ap()
    bq = nc.dram_tensor("bq", [1, D], F32, kind="ExternalInput").ap()
    bk = nc.dram_tensor("bk", [1, KVH * HD], F32, kind="ExternalInput").ap()
    bv = nc.dram_tensor("bv", [1, KVH * HD], F32, kind="ExternalInput").ap()
    bo = nc.dram_tensor("bo", [1, D], F32, kind="ExternalInput").ap()
    fb_pc = nc.dram_tensor("fb_pc", [128, DC], F32, kind="ExternalInput").ap()
    fws_pc = nc.dram_tensor("fws_pc", [128, DC], F32, kind="ExternalInput").ap()
    fis_bc = nc.dram_tensor("fis_bc", [128, 1], F32, kind="ExternalInput").ap()
    ident = nc.dram_tensor("ident", [128, 128], BF16, kind="ExternalInput").ap()
    y = nc.dram_tensor("y", [OWN, D], F32, kind="ExternalOutput").ap()

    kT_d = nc.dram_tensor("kT_d", [KVH, 128, T], BF16, kind="Internal").ap()
    v_d = nc.dram_tensor("v_d", [NBT, 128, KVH * HD], BF16, kind="Internal").ap()
    h2_d = nc.dram_tensor("h2_d", [OWN, D], F32, kind="Internal").ap()

    es = contextlib.ExitStack()
    with es:
        identb = es.enter_context(nc.sbuf_tensor("identb", [128, 128], BF16))
        ones_col = es.enter_context(nc.sbuf_tensor("ones_col", [128, 1], BF16))
        ones_row = es.enter_context(nc.sbuf_tensor("ones_row", [1, 512], BF16))
        ones_row_f = es.enter_context(nc.sbuf_tensor("ones_row_f", [1, 128], F32))
        fb_sb = es.enter_context(nc.sbuf_tensor("fb_sb", [128, DC], F32))
        fws_sb = es.enter_context(nc.sbuf_tensor("fws_sb", [128, DC], F32))
        nfws_sb = es.enter_context(nc.sbuf_tensor("nfws_sb", [128, DC], F32))
        fis_sb = es.enter_context(nc.sbuf_tensor("fis_sb", [128, 1], F32))
        recip_sb = es.enter_context(nc.sbuf_tensor("recip_sb", [128, DC], F32))
        wsis_sb = es.enter_context(nc.sbuf_tensor("wsis_sb", [128, DC], F32))
        lconst = es.enter_context(nc.sbuf_tensor("lconst", [128, 8], F32))

        sems = {}
        for name in SEMN:
            sems[name] = es.enter_context(nc.semaphore(name))
        block = es.enter_context(nc.Block())
        P = Prog(sems)

        def dma(queue_eng, sem_name, out, in_, deps=()):
            P.wait(queue_eng, deps)
            P.cnt[sem_name] += 16
            v = P.cnt[sem_name]
            h = sems[sem_name]
            P.emit(queue_eng, lambda e, out=out, in_=in_, h=h:
                   e.dma_start(out=out, in_=in_).then_inc(h, 16))
            return v

        def op(eng, sem_name, fn, deps=(), selfwait=True):
            if selfwait and eng in ("vector", "scalar", "gpsimd"):
                deps = list(deps) + [(sem_name, P.cnt[sem_name])]
            P.wait(eng, deps)
            P.cnt[sem_name] += 1
            v = P.cnt[sem_name]
            h = sems[sem_name]
            P.emit(eng, lambda e, fn=fn, h=h: fn(e).then_inc(h, 1))
            return v

        def waiter(eng, sem_name, v):
            if v > 0:
                P.emit(eng, lambda e, h=sems[sem_name], v=v: e.wait_ge(h, v))

        v_id = dma("sync", "QS", identb[:], ident)
        dma("sync", "QS", fb_sb[:], fb_pc)
        v_fws = dma("sync", "QS", fws_sb[:], fws_pc)
        v_fis = dma("sync", "QS", fis_sb[:], fis_bc)
        op("vector", "DV", lambda e: e.memset(ones_col[:], 1.0))
        op("vector", "DV", lambda e: e.memset(ones_row[:], 1.0))
        v_ones = op("vector", "DV", lambda e: e.memset(ones_row_f[:], 1.0))
        v_wsis = op("vector", "DV",
                    lambda e: e.tensor_single_scalar(wsis_sb[:], fws_sb[:], fis_sb[:], ALU.mult),
                    deps=[("QS", v_fis)])
        v_recip = op("vector", "DV", lambda e: e.reciprocal(recip_sb[:], wsis_sb[:]),
                     deps=[("DV", v_wsis)])
        v_nfws = op("vector", "DV",
                    lambda e: e.tensor_single_scalar(nfws_sb[:], fws_sb[:], -1.0, ALU.mult),
                    deps=[("QS", v_fws)])
        P.wait("tensor", [("QS", v_id)])

        def ladder_biases(kind, bits):
            delta = 0.0
            out = []
            for (t, engine) in bits:
                th = float(t) if kind == "floor" else float(t) - 0.5
                thr = th - delta
                if engine == "sc":
                    out.append(-thr)
                    delta -= float(t) / 2.0
            return out

        FLOOR_BITS = [(128.0, "sc"), (64.0, "dve"), (32.0, "sc"), (16.0, "dve"),
                      (8.0, "sc"), (4.0, "dve"), (2.0, "sc"), (1.0, "dve")]
        ROUND_BITS = [(4.0, "sc"), (2.0, "dve"), (1.0, "sc")]
        _cvals = ladder_biases("floor", FLOOR_BITS) + ladder_biases("round", ROUND_BITS)
        cmap = {}
        v_lconst = 0
        for _i, _v in enumerate(_cvals):
            v_lconst = op("vector", "DV",
                          lambda e, _i=_i, _v=_v: e.memset(lconst[:, _i:_i + 1], _v))
            cmap[_v] = lconst[:, _i:_i + 1]

        def ln_stats(xt_ap, sq_scr_ap, st, deps_dv, deps_sc):
            v1 = op("vector", "DV", lambda e: e.reduce_sum(st["ssum"], xt_ap, axis=AX),
                    deps=deps_dv)
            v2 = op("scalar", "SC",
                    lambda e: e.activation(sq_scr_ap, xt_ap, AF.Square, accum_out=st["ssq"]),
                    deps=deps_sc)
            op("vector", "DV",
               lambda e: e.tensor_single_scalar(st["negm"], st["ssum"], -1.0 / D, ALU.mult),
               deps=[("DV", v1)])
            op("vector", "DV",
               lambda e: e.tensor_scalar(st["vare"], st["ssq"], 1.0 / D, EPS, ALU.mult, ALU.add),
               deps=[("SC", v2)])
            op("vector", "DV",
               lambda e: e.tensor_tensor(st["msq"], st["negm"], st["negm"], ALU.mult))
            v5 = op("vector", "DV",
                    lambda e: e.tensor_tensor(st["vare"], st["vare"], st["msq"], ALU.subtract))
            v6 = op("scalar", "SC", lambda e: e.activation(st["sd"], st["vare"], AF.Sqrt),
                    deps=[("DV", v5)])
            v_rstd = op("vector", "DV", lambda e: e.reciprocal(st["rstd"], st["sd"]),
                        deps=[("SC", v6)])
            return v_rstd

        def emit_ladder(kind, a_ap, r_ap, g_ap, out_ap, bits, deps):
            delta = 0.0
            cur = a_ap
            vr = list(deps)
            for (t, engine) in bits:
                th = float(t) if kind == "floor" else float(t) - 0.5
                thr = th - delta
                if engine == "dve":
                    vg = op("vector", "DV",
                            lambda e, cur=cur, thr=thr, t=t: e.tensor_scalar(
                                g_ap, cur, float(thr), -float(t), ALU.is_ge, ALU.mult),
                            deps=vr)
                    vnew = op("vector", "DV",
                              lambda e, cur=cur: e.tensor_tensor(r_ap, cur, g_ap, ALU.add),
                              deps=[("DV", vg)])
                else:
                    bias_ap = cmap[-float(thr)]
                    vs = op("scalar", "SC",
                            lambda e, cur=cur, bias_ap=bias_ap: e.activation(
                                g_ap, cur, AF.Sign, bias=bias_ap),
                            deps=list(vr) + [("DV", v_lconst)])
                    vnew = op("vector", "DV",
                              lambda e, cur=cur, t=t: e.scalar_tensor_tensor(
                                  r_ap, g_ap, -float(t) / 2.0, cur, ALU.mult, ALU.add),
                              deps=[("SC", vs)])
                    delta -= float(t) / 2.0
                cur = r_ap
                vr = [("DV", vnew)]
            vfin = op("vector", "DV",
                      lambda e, delta=delta: e.scalar_tensor_tensor(
                          out_ap, r_ap, float(delta), a_ap, ALU.add, ALU.subtract),
                      deps=vr)
            return vfin

        with contextlib.ExitStack() as p1:
            xt = p1.enter_context(nc.sbuf_tensor("p1_xt", [128, 2, D], F32))
            u_tok = p1.enter_context(nc.sbuf_tensor("p1_u", [128, D], BF16))
            uT_half = p1.enter_context(nc.sbuf_tensor("p1_uT", [128, DC, 2048], BF16))
            wkv_sb = p1.enter_context(nc.sbuf_tensor("p1_wkv", [128, 2, DC, 128], BF16))
            kv_stage = p1.enter_context(nc.sbuf_tensor("p1_st", [128, 2, 1024], BF16))
            vt_stage = p1.enter_context(nc.sbuf_tensor("p1_vt", [128, 2, 8, 128], BF16))
            stt = {k: p1.enter_context(nc.sbuf_tensor("p1_s_" + k, [128, 1], F32))
                   for k in ("ssum", "ssq", "negm", "msq", "vare", "sd", "rstd")}
            stt = {k: v[:] for k, v in stt.items()}
            bk_b = p1.enter_context(nc.sbuf_tensor("bk_b", [1, KVH * HD], BF16))
            bv_b = p1.enter_context(nc.sbuf_tensor("bv_b", [1, KVH * HD], BF16))
            psA = p1.enter_context(nc.psum_tensor("p1_psA", [128, 2, 1024], F32))
            psT = p1.enter_context(nc.psum_tensor("p1_psT", [128, 1024], BF16))
            dma("gpsimd", "QG", bk_b[:], bk)
            v_bias = dma("gpsimd", "QG", bv_b[:], bv)

            dv_u_done = [0, 0]
            pe_kv = [0, 0]
            dv_kv_evac = [0, 0]
            kv_dma = [0, 0]
            vt_dma = [0, 0]

            for half in range(2):
                pe_u_guard = P.cnt["PE"]
                base_t = half * 16
                for i in range(16):
                    t = base_t + i
                    buf = i % 2
                    vx = dma("sync", "QS", xt[:, buf, :], xb[t * 128:(t + 1) * 128, :],
                             deps=[("DV", dv_u_done[buf])])
                    v_rstd = ln_stats(xt[:, buf, :], u_tok[:], stt,
                                      deps_dv=[("QS", vx)],
                                      deps_sc=[("QS", vx), ("PE", pe_u_guard),
                                               ("DV", dv_u_done[1 - buf])])
                    vu = op("vector", "DV",
                            lambda e, buf=buf: e.tensor_scalar(
                                u_tok[:], xt[:, buf, :], stt["negm"], stt["rstd"],
                                ALU.add, ALU.mult),
                            deps=[("DV", v_rstd), ("PE", pe_u_guard)])
                    dv_u_done[buf] = vu
                    for g in range(4):
                        for k in range(8):
                            dc = g * 8 + k
                            d_ = [("DV", vu)] if (g == 0 and k == 0) else []
                            pe_v = op("tensor", "PE",
                                      lambda e, dc=dc, k=k: e.transpose(
                                          psT[:, k * 128:(k + 1) * 128],
                                          u_tok[:, dc * 128:(dc + 1) * 128], identb[:]),
                                      deps=d_)
                        dv_e = op("vector", "DV",
                                  lambda e, g=g, i=i: e.tensor_copy(
                                      uT_half[:, g * 8:(g + 1) * 8, i * 128:(i + 1) * 128],
                                      psT[:]),
                                  deps=[("PE", pe_v)])
                        waiter("tensor", "DV", dv_e)
                    pe_u_guard = P.cnt["PE"]

                for oc in range(16):
                    wbuf = oc % 2
                    wsrc = wk if oc < 8 else wv
                    col0 = (oc % 8) * 128
                    vw = dma("gpsimd", "QG", wkv_sb[:, wbuf, :, :],
                             wsrc[:, col0:col0 + 128].rearrange("(c p) n -> p c n", p=128),
                             deps=[("PE", pe_kv[wbuf])])
                    for nq in range(2):
                        pbuf = nq % 2
                        first_deps = [("QG", vw), ("DV", dv_kv_evac[pbuf])]
                        pe_last = 0
                        for dc in range(DC):
                            for s in range(2):
                                pe_last = op("tensor", "PE",
                                             lambda e, dc=dc, s=s, nq=nq, pbuf=pbuf, wbuf=wbuf: e.matmul(
                                                 psA[:, pbuf, s * 512:(s + 1) * 512],
                                                 wkv_sb[:, wbuf, dc, :],
                                                 uT_half[:, dc, nq * 1024 + s * 512:nq * 1024 + (s + 1) * 512],
                                                 start=(dc == 0), stop=False),
                                             deps=(first_deps if (dc == 0 and s == 0) else []))
                        bias_src = bk_b if oc < 8 else bv_b
                        for s in range(2):
                            d_ = ([("QG", v_bias), ("DV", v_ones)]
                                  if (half == 0 and oc == 0 and nq == 0 and s == 0) else [])
                            pe_last = op("tensor", "PE",
                                         lambda e, s=s, pbuf=pbuf, bias_src=bias_src, col0=col0: e.matmul(
                                             psA[:, pbuf, s * 512:(s + 1) * 512],
                                             bias_src[0:1, col0:col0 + 128],
                                             ones_row[0:1, :],
                                             start=False, stop=(s == 1)),
                                         deps=d_)
                        pe_kv[wbuf] = pe_last
                        dv_e = op("vector", "DV",
                                  lambda e, pbuf=pbuf: e.tensor_copy(
                                      kv_stage[:, pbuf, :], psA[:, pbuf, :]),
                                  deps=[("PE", pe_last), ("QS", kv_dma[pbuf])])
                        dv_kv_evac[pbuf] = dv_e
                        if oc < 8:
                            kv_dma[pbuf] = dma(
                                "sync", "QS",
                                kT_d[oc, :, half * 2048 + nq * 1024:half * 2048 + (nq + 1) * 1024],
                                kv_stage[:, pbuf, :],
                                deps=[("DV", dv_e)])
                        else:
                            pe_tv = 0
                            for k in range(8):
                                pe_tv = op("tensor", "PE",
                                           lambda e, k=k, pbuf=pbuf: e.transpose(
                                               psT[:, k * 128:(k + 1) * 128],
                                               kv_stage[:, pbuf, k * 128:(k + 1) * 128],
                                               identb[:]),
                                           deps=([("DV", dv_e)] if k == 0 else []))
                            kv_dma[pbuf] = P.cnt["QS"]
                            dv_v = op("vector", "DV",
                                      lambda e, pbuf=pbuf: e.tensor_copy(
                                          vt_stage[:, pbuf, :, :], psT[:]),
                                      deps=[("PE", pe_tv), ("QS", vt_dma[pbuf])])
                            waiter("tensor", "DV", dv_v)
                            kt0 = half * 16 + nq * 8
                            vt_dma[pbuf] = dma(
                                "sync", "QS",
                                v_d[kt0:kt0 + 8, :, (oc - 8) * 128:(oc - 7) * 128]
                                .rearrange("t p n -> p t n"),
                                vt_stage[:, pbuf, :, :],
                                deps=[("DV", dv_v)])
            P.join()
            P.flush(block)

        qT_cm = nc.sbuf_tensor("qT", [128, QH, OWN], BF16)
        qT = qT_cm.__enter__()
        with contextlib.ExitStack() as p2:
            xt = p2.enter_context(nc.sbuf_tensor("p2_xt", [128, 2, D], F32))
            u_tok = p2.enter_context(nc.sbuf_tensor("p2_u", [128, D], BF16))
            uT_own = p2.enter_context(nc.sbuf_tensor("p2_uT", [128, DC, OWN], BF16))
            wq_sb = p2.enter_context(nc.sbuf_tensor("p2_wq", [128, 2, DC, 128], BF16))
            stt = {k: p2.enter_context(nc.sbuf_tensor("p2_s_" + k, [128, 1], F32))
                   for k in ("ssum", "ssq", "negm", "msq", "vare", "sd", "rstd")}
            stt = {k: v[:] for k, v in stt.items()}
            bq_b = p2.enter_context(nc.sbuf_tensor("bq_b", [1, D], BF16))
            psA = p2.enter_context(nc.psum_tensor("p2_psA", [128, 2, 1024], F32))
            psT = p2.enter_context(nc.psum_tensor("p2_psT", [128, 1024], BF16))
            v_bq = dma("gpsimd", "QG", bq_b[:], bq)

            dv_u_done = [0, 0]
            pe_u_guard = 0
            for i in range(NQT):
                buf = i % 2
                vx = dma("sync", "QS", xt[:, buf, :], xb[i * 128:(i + 1) * 128, :],
                         deps=[("DV", dv_u_done[buf])])
                v_rstd = ln_stats(xt[:, buf, :], u_tok[:], stt,
                                  deps_dv=[("QS", vx)],
                                  deps_sc=[("QS", vx), ("PE", pe_u_guard),
                                           ("DV", dv_u_done[1 - buf])])
                vu = op("vector", "DV",
                        lambda e, buf=buf: e.tensor_scalar(
                            u_tok[:], xt[:, buf, :], stt["negm"], stt["rstd"],
                            ALU.add, ALU.mult),
                        deps=[("DV", v_rstd), ("PE", pe_u_guard)])
                dv_u_done[buf] = vu
                for g in range(4):
                    for k in range(8):
                        dc = g * 8 + k
                        d_ = [("DV", vu)] if (g == 0 and k == 0) else []
                        pe_v = op("tensor", "PE",
                                  lambda e, dc=dc, k=k: e.transpose(
                                      psT[:, k * 128:(k + 1) * 128],
                                      u_tok[:, dc * 128:(dc + 1) * 128], identb[:]),
                                  deps=d_)
                    dv_e = op("vector", "DV",
                              lambda e, g=g, i=i: e.tensor_copy(
                                  uT_own[:, g * 8:(g + 1) * 8, i * 128:(i + 1) * 128],
                                  psT[:]),
                              deps=[("PE", pe_v)])
                    waiter("tensor", "DV", dv_e)
                pe_u_guard = P.cnt["PE"]

            pe_q = [0, 0]
            dv_q_evac = [0, 0]
            for oc in range(DC):
                wbuf = oc % 2
                vw = dma("gpsimd", "QG", wq_sb[:, wbuf, :, :],
                         wq[:, oc * 128:(oc + 1) * 128].rearrange("(c p) n -> p c n", p=128),
                         deps=[("PE", pe_q[wbuf])])
                pbuf = oc % 2
                first_deps = [("QG", vw), ("DV", dv_q_evac[pbuf])]
                pe_last = 0
                for dc in range(DC):
                    for s in range(2):
                        pe_last = op("tensor", "PE",
                                     lambda e, dc=dc, s=s, pbuf=pbuf, wbuf=wbuf: e.matmul(
                                         psA[:, pbuf, s * 512:(s + 1) * 512],
                                         wq_sb[:, wbuf, dc, :],
                                         uT_own[:, dc, s * 512:(s + 1) * 512],
                                         start=(dc == 0), stop=False),
                                     deps=(first_deps if (dc == 0 and s == 0) else []))
                for s in range(2):
                    pe_last = op("tensor", "PE",
                                 lambda e, s=s, pbuf=pbuf, oc=oc: e.matmul(
                                     psA[:, pbuf, s * 512:(s + 1) * 512],
                                     bq_b[0:1, oc * 128:(oc + 1) * 128],
                                     ones_row[0:1, :],
                                     start=False, stop=(s == 1)),
                                 deps=([("QG", v_bq)] if (oc == 0 and s == 0) else []))
                pe_q[wbuf] = pe_last
                dv_q_evac[pbuf] = op("vector", "DV",
                                     lambda e, pbuf=pbuf, oc=oc: e.tensor_copy(
                                         qT[:, oc, :], psA[:, pbuf, :]),
                                     deps=[("PE", pe_last)])
            P.join()
            P.flush(block)

        AT_cm = nc.sbuf_tensor("AT", [128, QH, OWN], BF16, side="right")
        AT = AT_cm.__enter__()
        with contextlib.ExitStack() as p3:
            kT_sb = p3.enter_context(nc.sbuf_tensor("p3_kT", [128, 2, T], BF16))
            v_tok = p3.enter_context(nc.sbuf_tensor("p3_v", [128, 2, NBT, 128], BF16))
            ptile = p3.enter_context(nc.sbuf_tensor("p3_p", [128, 2, 1024], BF16))
            sqs = p3.enter_context(nc.sbuf_tensor("p3_sqs", [128, T], BF16))
            mk_sb = p3.enter_context(nc.sbuf_tensor("p3_mk", [1, 16], F32))
            cb = p3.enter_context(nc.sbuf_tensor("p3_cb", [128, 4], F32))
            sig_sb = p3.enter_context(nc.sbuf_tensor("p3_sig", [1, 1024], F32))
            rs_sb = p3.enter_context(nc.sbuf_tensor("p3_rs", [1, 1024], F32))
            bc_sb = p3.enter_context(nc.sbuf_tensor("p3_bc", [128, 1024], F32))
            psS = p3.enter_context(nc.psum_tensor("p3_psS", [128, 2, 1024], F32))
            psA_ = p3.enter_context(nc.psum_tensor("p3_psA", [128, 1024], F32))
            psG = p3.enter_context(nc.psum_tensor("p3_psG", [128, 1024], F32))

            pe_group_done = [0, 0]
            pe_sqs_read = 0
            dv_head_tail = 0
            sc_exp = [0, 0]
            pe_pv_sig = [0, 0]
            for j in range(KVH):
                gbuf = j % 2
                vk = dma("sync", "QS", kT_sb[:, gbuf, :], kT_d[j, :, :],
                         deps=[("PE", pe_group_done[gbuf])])
                vv = vk
                for q4 in range(4):
                    vv = dma("sync", "QS",
                             v_tok[:, gbuf, q4 * 8:(q4 + 1) * 8, :],
                             v_d[q4 * 8:(q4 + 1) * 8, :, j * 128:(j + 1) * 128]
                             .rearrange("t p n -> p t n"))
                dv_sq = op("vector", "DV",
                           lambda e, gbuf=gbuf: e.tensor_tensor(
                               sqs[:], kT_sb[:, gbuf, :], kT_sb[:, gbuf, :], ALU.mult),
                           deps=[("QS", vk), ("PE", pe_sqs_read)])
                dvm = [0, 0]
                for s in range(8):
                    slot = s % 2
                    d_ = [("DV", dv_sq)] if s == 0 else []
                    waiter("tensor", "DV", dvm[slot])
                    pe_n = op("tensor", "PE",
                              lambda e, s=s, slot=slot: e.matmul(
                                  psG[0:1, slot * 512:(slot + 1) * 512],
                                  ones_col[:], sqs[:, s * 512:(s + 1) * 512],
                                  start=True, stop=True),
                              deps=d_)
                    dvm[slot] = op("vector", "DV",
                                   lambda e, s=s, slot=slot: e.reduce_max(
                                       mk_sb[0:1, s:s + 1],
                                       psG[0:1, slot * 512:(slot + 1) * 512], axis=AX),
                                   deps=[("PE", pe_n)])
                op("vector", "DV",
                   lambda e: e.reduce_max(mk_sb[0:1, 8:9], mk_sb[0:1, 0:8], axis=AX))

                dv_cb = 0
                for hh in range(GQ):
                    h = j * GQ + hh
                    dv_qsq = op("vector", "DV",
                                lambda e, h=h: e.tensor_tensor(
                                    sqs[:, 0:1024], qT[:, h, :], qT[:, h, :], ALU.mult),
                                deps=[("PE", pe_sqs_read)])
                    pe_n = 0
                    for s in range(2):
                        pe_n = op("tensor", "PE",
                                  lambda e, s=s: e.matmul(
                                      psG[0:1, s * 512:(s + 1) * 512],
                                      ones_col[:], sqs[:, s * 512:(s + 1) * 512],
                                      start=True, stop=True),
                                  deps=([("DV", dv_qsq)] if s == 0 else []))
                    pe_sqs_read = pe_n
                    op("vector", "DV",
                       lambda e: e.reduce_max(mk_sb[0:1, 9:10], psG[0:1, 0:512], axis=AX),
                       deps=[("PE", pe_n)])
                    op("vector", "DV",
                       lambda e: e.reduce_max(mk_sb[0:1, 10:11], psG[0:1, 512:1024], axis=AX))
                    op("vector", "DV",
                       lambda e: e.reduce_max(mk_sb[0:1, 11:12], mk_sb[0:1, 9:11], axis=AX))
                    dv_c1 = op("vector", "DV",
                               lambda e: e.tensor_tensor(mk_sb[0:1, 12:13], mk_sb[0:1, 11:12],
                                                         mk_sb[0:1, 8:9], ALU.add))
                    dv_c3 = op("vector", "DV",
                               lambda e: e.tensor_single_scalar(
                                   mk_sb[0:1, 14:15], mk_sb[0:1, 12:13], -SCALE / 2.0, ALU.mult),
                               deps=[("DV", dv_c1)])
                    pe_cb = op("tensor", "PE",
                               lambda e: e.matmul(psG[:, 0:1], ones_row_f[0:1, :],
                                                  mk_sb[0:1, 14:15], start=True, stop=True),
                               deps=[("DV", dv_c3)])
                    dv_cb = op("vector", "DV",
                               lambda e, hh=hh: e.tensor_copy(cb[:, hh:hh + 1], psG[:, 0:1]),
                               deps=[("PE", pe_cb)])
                waiter("tensor", "DV", dv_cb)
                dv_b_prev = 0

                for hh in range(GQ):
                    h = j * GQ + hh
                    def emit_s(kc):
                        sb_ = kc % 2
                        pe_s = 0
                        for s in range(2):
                            d_ = ([("SC", sc_exp[sb_])] if s == 0 else [])
                            pe_s = op("tensor", "PE",
                                      lambda e, kc=kc, s=s, sb_=sb_, gbuf=gbuf, h=h: e.matmul(
                                          psS[:, sb_, s * 512:(s + 1) * 512],
                                          kT_sb[:, gbuf, kc * 128:(kc + 1) * 128],
                                          qT[:, h, s * 512:(s + 1) * 512],
                                          start=True, stop=True),
                                      deps=d_)
                        return pe_s

                    def emit_exp(kc, pe_s):
                        sb_ = kc % 2
                        d_exp = [("PE", pe_s), ("PE", pe_pv_sig[sb_])]
                        if kc == 0:
                            d_exp.append(("DV", dv_cb))
                        vexp = op("scalar", "SC",
                                  lambda e, sb_=sb_, hh=hh: e.activation(
                                      ptile[:, sb_, :], psS[:, sb_, :], AF.Exp,
                                      bias=cb[:, hh:hh + 1], scale=SCALE),
                                  deps=d_exp, selfwait=False)
                        sc_exp[sb_] = vexp
                        return vexp

                    def emit_pv(kc, vexp):
                        sb_ = kc % 2
                        for s in range(2):
                            d_ = []
                            if s == 0:
                                d_ = [("SC", vexp)]
                                if kc == 0:
                                    d_ += [("QS", vv), ("DV", dv_head_tail)]
                            op("tensor", "PE",
                               lambda e, kc=kc, s=s, sb_=sb_, gbuf=gbuf: e.matmul(
                                   psA_[:, s * 512:(s + 1) * 512],
                                   v_tok[:, gbuf, kc, :],
                                   ptile[:, sb_, s * 512:(s + 1) * 512],
                                   start=(kc == 0), stop=(kc == NBT - 1)),
                               deps=d_)
                        pe_g = 0
                        for s in range(2):
                            d2_ = ([("DV", dv_b_prev)] if (kc == 0 and s == 0) else [])
                            pe_g = op("tensor", "PE",
                                      lambda e, kc=kc, s=s, sb_=sb_: e.matmul(
                                          psG[0:1, s * 512:(s + 1) * 512],
                                          ones_col[:],
                                          ptile[:, sb_, s * 512:(s + 1) * 512],
                                          start=(kc == 0), stop=(kc == NBT - 1)),
                                      deps=d2_)
                        pe_pv_sig[sb_] = pe_g

                    pe_s_arr = {0: emit_s(0), 1: emit_s(1)}
                    for kc in range(NBT):
                        vexp = emit_exp(kc, pe_s_arr[kc])
                        if kc + 2 < NBT:
                            pe_s_arr[kc + 2] = emit_s(kc + 2)
                        emit_pv(kc, vexp)
                    pe_head_mms = P.cnt["PE"]
                    dv_s1 = op("vector", "DV",
                               lambda e: e.tensor_copy(sig_sb[:], psG[0:1, :]),
                               deps=[("PE", pe_head_mms)])
                    dv_s2 = op("vector", "DV",
                               lambda e: e.reciprocal(rs_sb[:], sig_sb[:]),
                               deps=[("DV", dv_s1)])
                    pe_b = 0
                    for s in range(2):
                        d_ = ([("DV", dv_s2)] if s == 0 else [])
                        pe_b = op("tensor", "PE",
                                  lambda e, s=s: e.matmul(
                                      psG[:, s * 512:(s + 1) * 512],
                                      ones_row_f[0:1, :],
                                      rs_sb[0:1, s * 512:(s + 1) * 512],
                                      start=True, stop=True),
                                  deps=d_)
                    dv_b = op("vector", "DV",
                              lambda e: e.tensor_copy(bc_sb[:], psG[:, :]),
                              deps=[("PE", pe_b)])
                    dv_b_prev = dv_b
                    dv_n = op("vector", "DV",
                              lambda e, h=h: e.tensor_tensor(
                                  AT[:, h, :], psA_[:], bc_sb[:], ALU.mult),
                              deps=[("DV", dv_b)])
                    dv_head_tail = dv_n
                pe_group_done[gbuf] = P.cnt["PE"]
            P.join()
            P.flush(block)

        qT_cm.__exit__(None, None, None)

        big_cm = nc.sbuf_tensor("big_tok", [128, NQT, D], BF16)
        big_tok = big_cm.__enter__()
        with contextlib.ExitStack() as p4:
            wo_sb = p4.enter_context(nc.sbuf_tensor("p4_wo", [128, 2, DC, 128], BF16))
            ot_stage = p4.enter_context(nc.sbuf_tensor("p4_ot", [128, 2, 1024], BF16))
            bo_b = p4.enter_context(nc.sbuf_tensor("bo_b", [1, D], BF16))
            psA = p4.enter_context(nc.psum_tensor("p4_psA", [128, 2, 1024], F32))
            psT = p4.enter_context(nc.psum_tensor("p4_psT", [128, 1024], BF16))
            v_bo = dma("gpsimd", "QG", bo_b[:], bo)

            pe_o = [0, 0]
            dv_o_evac = [0, 0]
            pe_ot_read = [0, 0]
            for oc in range(DC):
                wbuf = oc % 2
                vw = dma("gpsimd", "QG", wo_sb[:, wbuf, :, :],
                         wo[:, oc * 128:(oc + 1) * 128].rearrange("(c p) n -> p c n", p=128),
                         deps=[("PE", pe_o[wbuf])])
                pbuf = oc % 2
                first_deps = [("QG", vw), ("DV", dv_o_evac[pbuf])]
                pe_last = 0
                for dc in range(DC):
                    for s in range(2):
                        pe_last = op("tensor", "PE",
                                     lambda e, dc=dc, s=s, pbuf=pbuf, wbuf=wbuf: e.matmul(
                                         psA[:, pbuf, s * 512:(s + 1) * 512],
                                         wo_sb[:, wbuf, dc, :],
                                         AT[:, dc, s * 512:(s + 1) * 512],
                                         start=(dc == 0), stop=False),
                                     deps=(first_deps if (dc == 0 and s == 0) else []))
                for s in range(2):
                    pe_last = op("tensor", "PE",
                                 lambda e, s=s, pbuf=pbuf, oc=oc: e.matmul(
                                     psA[:, pbuf, s * 512:(s + 1) * 512],
                                     bo_b[0:1, oc * 128:(oc + 1) * 128],
                                     ones_row[0:1, :],
                                     start=False, stop=(s == 1)),
                                 deps=([("QG", v_bo)] if (oc == 0 and s == 0) else []))
                pe_o[wbuf] = pe_last
                dv_e = op("vector", "DV",
                          lambda e, pbuf=pbuf: e.tensor_copy(ot_stage[:, pbuf, :],
                                                             psA[:, pbuf, :]),
                          deps=[("PE", pe_last), ("PE", pe_ot_read[pbuf])])
                dv_o_evac[pbuf] = dv_e
                pe_t = 0
                for k in range(8):
                    pe_t = op("tensor", "PE",
                              lambda e, k=k, pbuf=pbuf: e.transpose(
                                  psT[:, k * 128:(k + 1) * 128],
                                  ot_stage[:, pbuf, k * 128:(k + 1) * 128],
                                  identb[:]),
                              deps=([("DV", dv_e)] if k == 0 else []))
                pe_ot_read[pbuf] = pe_t
                dv_t = op("vector", "DV",
                          lambda e, oc=oc: e.tensor_copy(
                              big_tok[:, :, oc * 128:(oc + 1) * 128], psT[:]),
                          deps=[("PE", pe_t)])
                waiter("tensor", "DV", dv_t)
            P.join()
            P.flush(block)

        AT_cm.__exit__(None, None, None)

        xiT_cm = nc.sbuf_tensor("xiT", [128, DC, OWN], BF16, side="right")
        xiT = xiT_cm.__enter__()
        with contextlib.ExitStack() as p4b:
            xtile = p4b.enter_context(nc.sbuf_tensor("p4b_x", [128, D], F32))
            h2t = p4b.enter_context(nc.sbuf_tensor("p4b_h2", [128, D], F32))
            sq4 = p4b.enter_context(nc.sbuf_tensor("p4b_sq", [128, D], BF16))
            xs_h = p4b.enter_context(nc.sbuf_tensor("p4b_xs", [128, HALF], F32))
            a_h = p4b.enter_context(nc.sbuf_tensor("p4b_a", [128, HALF], F32))
            r_h = p4b.enter_context(nc.sbuf_tensor("p4b_r", [128, HALF], F32))
            ns_h = p4b.enter_context(nc.sbuf_tensor("p4b_ns", [128, HALF], BF16))
            xi_h = p4b.enter_context(nc.sbuf_tensor("p4b_xi", [128, HALF], BF16))
            stt = {k: p4b.enter_context(nc.sbuf_tensor("p4b_s_" + k, [128, 1], F32))
                   for k in ("ssum", "ssq", "negm", "msq", "vare", "sd", "rstd")}
            stt = {k: v[:] for k, v in stt.items()}
            rstd3f = p4b.enter_context(nc.sbuf_tensor("p4b_r3f", [128, 1], F32))
            psT = p4b.enter_context(nc.psum_tensor("p4b_psT", [128, 1024], BF16))

            pe_xi_read = 0
            for qt in range(NQT):
                P.join()
                vx = dma("sync", "QS", xtile[:], xb[qt * 128:(qt + 1) * 128, :])
                v_rstd2 = ln_stats(big_tok[:, qt, :], sq4[:], stt, deps_dv=[], deps_sc=[])
                dv_y1 = op("vector", "DV",
                           lambda e, qt=qt: e.tensor_scalar(
                               h2t[:], big_tok[:, qt, :], stt["negm"], stt["rstd"],
                               ALU.add, ALU.mult),
                           deps=[("DV", v_rstd2)])
                dv_h2 = op("vector", "DV",
                           lambda e: e.tensor_tensor(h2t[:], h2t[:], xtile[:], ALU.add),
                           deps=[("DV", dv_y1), ("QS", vx)])
                dma("sync", "QS", h2_d[qt * 128:(qt + 1) * 128, :], h2t[:],
                    deps=[("DV", dv_h2)])
                v_rstd3 = ln_stats(h2t[:], sq4[:], stt,
                                   deps_dv=[("DV", dv_h2)],
                                   deps_sc=[("DV", dv_h2), ("DV", dv_y1)])
                dv_r3f = op("vector", "DV",
                            lambda e: e.tensor_tensor(rstd3f[:], stt["rstd"], fis_sb[:],
                                                      ALU.mult),
                            deps=[("DV", v_rstd3)])
                for dh in range(2):
                    o0 = dh * HALF
                    dv_xs = op("vector", "DV",
                               lambda e, o0=o0: e.tensor_scalar(
                                   xs_h[:], h2t[:, o0:o0 + HALF], stt["negm"], rstd3f[:],
                                   ALU.add, ALU.mult),
                               deps=[("DV", dv_r3f)])
                    sc_a = op("scalar", "SC",
                              lambda e: e.activation(a_h[:], xs_h[:], AF.Abs),
                              deps=[("DV", dv_xs)])
                    sc_ns = op("scalar", "SC",
                              lambda e: e.activation(ns_h[:], xs_h[:], AF.Sign, scale=-1.0),
                              deps=[("DV", dv_xs)])
                    vneg = emit_ladder("floor", a_h[:], r_h[:], xs_h[:], xs_h[:],
                                       FLOOR_BITS, [("SC", sc_a), ("SC", sc_ns)])
                    dv_xi = op("vector", "DV",
                               lambda e: e.tensor_tensor(xi_h[:], xs_h[:], ns_h[:], ALU.mult),
                               deps=[("DV", vneg), ("PE", pe_xi_read)])
                    for g in range(2):
                        for k in range(8):
                            d_ = [("DV", dv_xi)] if (g == 0 and k == 0) else []
                            pe_v = op("tensor", "PE",
                                      lambda e, g=g, k=k: e.transpose(
                                          psT[:, k * 128:(k + 1) * 128],
                                          xi_h[:, (g * 8 + k) * 128:(g * 8 + k + 1) * 128],
                                          identb[:]),
                                      deps=d_)
                        dv_e = op("vector", "DV",
                                  lambda e, g=g, dh=dh, qt=qt: e.tensor_copy(
                                      xiT[:, dh * 16 + g * 8:dh * 16 + (g + 1) * 8,
                                          qt * 128:(qt + 1) * 128],
                                      psT[:]),
                                  deps=[("PE", pe_v)])
                        waiter("tensor", "DV", dv_e)
                    pe_xi_read = pe_v
            P.join()
            P.flush(block)

        big_cm.__exit__(None, None, None)

        ffn_cm = nc.sbuf_tensor("ffn_tok", [128, NQT, D], BF16)
        ffn_tok = ffn_cm.__enter__()
        with contextlib.ExitStack() as p5:
            fw_sb = p5.enter_context(nc.sbuf_tensor("p5_fw", [128, D], F32))
            a_h = p5.enter_context(nc.sbuf_tensor("p5_a", [128, HALF], F32))
            r_h = p5.enter_context(nc.sbuf_tensor("p5_r", [128, HALF], F32))
            g_h = p5.enter_context(nc.sbuf_tensor("p5_g", [128, HALF], F32))
            ns_h = p5.enter_context(nc.sbuf_tensor("p5_ns", [128, HALF], BF16))
            wi_h = p5.enter_context(nc.sbuf_tensor("p5_wi", [128, HALF], BF16))
            wT_sb = p5.enter_context(nc.sbuf_tensor("p5_wT", [128, 2, DC, 128], BF16))
            ft_stage = p5.enter_context(nc.sbuf_tensor("p5_ft", [128, 2, 1024], BF16))
            psA = p5.enter_context(nc.psum_tensor("p5_psA", [128, 2, 1024], F32))
            psT = p5.enter_context(nc.psum_tensor("p5_psT", [128, 1024], BF16))

            sc_fw_done = 0
            pe_wi_read = 0
            dv_wi_last = 0
            pe_wT = [0, 0]
            dv_f_evac = [0, 0]
            pe_ft_read = [0, 0]
            for oc in range(DC):
                vw = dma("gpsimd", "QG", fw_sb[:], fw[oc * 128:(oc + 1) * 128, :],
                         deps=[("SC", sc_fw_done)])
                wbuf = oc % 2
                for dh in range(2):
                    o0 = dh * HALF
                    sc_a = op("scalar", "SC",
                              lambda e, oc=oc, o0=o0: e.activation(
                                  a_h[:], fw_sb[:, o0:o0 + HALF], AF.Abs,
                                  scale=fws_sb[:, oc:oc + 1]),
                              deps=[("QG", vw), ("QS", v_fws), ("DV", dv_wi_last)])
                    sc_ns = op("scalar", "SC",
                               lambda e, oc=oc, o0=o0: e.activation(
                                   ns_h[:], fw_sb[:, o0:o0 + HALF], AF.Sign,
                                   scale=nfws_sb[:, oc:oc + 1]),
                               deps=[("DV", v_nfws), ("DV", dv_wi_last)])
                    sc_fw_done = sc_ns
                    vneg = emit_ladder("round", a_h[:], r_h[:], g_h[:], g_h[:],
                                       ROUND_BITS, [("SC", sc_a), ("SC", sc_ns)])
                    dv_wi = op("vector", "DV",
                               lambda e: e.tensor_tensor(wi_h[:], g_h[:], ns_h[:], ALU.mult),
                               deps=[("DV", vneg), ("PE", pe_wi_read)])
                    dv_wi_last = dv_wi
                    for g in range(2):
                        for k in range(8):
                            d_ = ([("DV", dv_wi)] if (g == 0 and k == 0) else [])
                            pe_v = op("tensor", "PE",
                                      lambda e, g=g, k=k: e.transpose(
                                          psT[:, k * 128:(k + 1) * 128],
                                          wi_h[:, (g * 8 + k) * 128:(g * 8 + k + 1) * 128],
                                          identb[:]),
                                      deps=d_)
                        dv_e = op("vector", "DV",
                                  lambda e, g=g, dh=dh, wbuf=wbuf: e.tensor_copy(
                                      wT_sb[:, wbuf, dh * 16 + g * 8:dh * 16 + (g + 1) * 8, :],
                                      psT[:]),
                                  deps=[("PE", pe_v)])
                        waiter("tensor", "DV", dv_e)
                    pe_wi_read = pe_v
                dv_wT = P.cnt["DV"]
                pbuf = oc % 2
                first_deps = [("DV", dv_wT), ("DV", dv_f_evac[pbuf]), ("PE", pe_wT[wbuf])]
                pe_last = 0
                for dc in range(DC):
                    for s in range(2):
                        pe_last = op("tensor", "PE",
                                     lambda e, dc=dc, s=s, pbuf=pbuf, wbuf=wbuf: e.matmul(
                                         psA[:, pbuf, s * 512:(s + 1) * 512],
                                         wT_sb[:, wbuf, dc, :],
                                         xiT[:, dc, s * 512:(s + 1) * 512],
                                         start=(dc == 0), stop=(dc == DC - 1)),
                                     deps=(first_deps if (dc == 0 and s == 0) else []))
                pe_wT[wbuf] = pe_last
                dv_e = op("vector", "DV",
                          lambda e, pbuf=pbuf, oc=oc: e.tensor_scalar(
                              ft_stage[:, pbuf, :], psA[:, pbuf, :],
                              fb_sb[:, oc:oc + 1], recip_sb[:, oc:oc + 1],
                              ALU.add, ALU.mult),
                          deps=[("PE", pe_last), ("DV", v_recip), ("PE", pe_ft_read[pbuf])])
                dv_f_evac[pbuf] = dv_e
                pe_t = 0
                for k in range(8):
                    pe_t = op("tensor", "PE",
                              lambda e, k=k, pbuf=pbuf: e.transpose(
                                  psT[:, k * 128:(k + 1) * 128],
                                  ft_stage[:, pbuf, k * 128:(k + 1) * 128],
                                  identb[:]),
                              deps=([("DV", dv_e)] if k == 0 else []))
                pe_ft_read[pbuf] = pe_t
                dv_t = op("vector", "DV",
                          lambda e, oc=oc: e.tensor_copy(
                              ffn_tok[:, :, oc * 128:(oc + 1) * 128], psT[:]),
                          deps=[("PE", pe_t)])
                waiter("tensor", "DV", dv_t)
            P.join()
            P.flush(block)

        xiT_cm.__exit__(None, None, None)

        with contextlib.ExitStack() as p5b:
            h2t = p5b.enter_context(nc.sbuf_tensor("p5b_h2", [128, 2, D], F32))
            yt = p5b.enter_context(nc.sbuf_tensor("p5b_y", [128, D], F32))
            sq5 = p5b.enter_context(nc.sbuf_tensor("p5b_sq", [128, D], BF16))
            stt = {k: p5b.enter_context(nc.sbuf_tensor("p5b_s_" + k, [128, 1], F32))
                   for k in ("ssum", "ssq", "negm", "msq", "vare", "sd", "rstd")}
            stt = {k: v[:] for k, v in stt.items()}

            v_out_final = 0
            prev_apply = 0
            prev_sq = 0
            for qt in range(NQT):
                buf = qt % 2
                vh = dma("sync", "QS", h2t[:, buf, :], h2_d[qt * 128:(qt + 1) * 128, :],
                         deps=[("DV", prev_apply)])
                v_rstd4 = ln_stats(ffn_tok[:, qt, :], sq5[:], stt,
                                   deps_dv=[], deps_sc=[("DV", prev_apply)])
                dv_y2 = op("vector", "DV",
                           lambda e, qt=qt: e.tensor_scalar(
                               yt[:], ffn_tok[:, qt, :], stt["negm"], stt["rstd"],
                               ALU.add, ALU.mult),
                           deps=[("DV", v_rstd4), ("QS", v_out_final)])
                dv_out = op("vector", "DV",
                            lambda e, buf=buf: e.tensor_tensor(yt[:], yt[:],
                                                               h2t[:, buf, :], ALU.add),
                            deps=[("DV", dv_y2), ("QS", vh)])
                prev_apply = dv_out
                v_out_final = dma("sync", "QS", y[qt * 128:(qt + 1) * 128, :], yt[:],
                                  deps=[("DV", dv_out)])
            waiter("sync", "QS", v_out_final)
            P.flush(block)

        ffn_cm.__exit__(None, None, None)

    return nc


def kernel(**inputs):
    x = np.asarray(inputs["L0_attn_x"])
    wq_ = np.asarray(inputs["L0_attn_w_q"])
    bq_ = np.asarray(inputs["L0_attn_b_q"]).reshape(1, -1)
    wk_ = np.asarray(inputs["L0_attn_w_k"])
    bk_ = np.asarray(inputs["L0_attn_b_k"]).reshape(1, -1)
    wv_ = np.asarray(inputs["L0_attn_w_v"])
    bv_ = np.asarray(inputs["L0_attn_b_v"]).reshape(1, -1)
    wo_ = np.asarray(inputs["L0_attn_w_o"])
    bo_ = np.asarray(inputs["L0_attn_b_o"]).reshape(1, -1)
    fw_ = np.asarray(inputs["L0_ffn_weight"])
    fb_ = np.asarray(inputs["L0_ffn_bias"])
    fws_ = np.asarray(inputs["L0_ffn_weight_scale"])
    fis_ = np.asarray(inputs["L0_ffn_input_scale"])

    fb_pc = np.ascontiguousarray(fb_.reshape(DC, 128).T)
    fws_pc = np.ascontiguousarray(fws_.reshape(DC, 128).T)
    fis_bc = np.ascontiguousarray(np.broadcast_to(fis_.reshape(1, 1), (128, 1)))
    identm = np.eye(128, dtype=ml_dtypes.bfloat16)

    nc = build_program()
    in_maps = []
    for c in range(8):
        b = c // 4
        q0 = (c % 4) * OWN
        xr = np.ascontiguousarray(np.roll(x[b], -q0, axis=0))
        in_maps.append(dict(
            xb=xr, wq=wq_, wk=wk_, wv=wv_, wo=wo_, fw=fw_,
            bq=bq_, bk=bk_, bv=bv_, bo=bo_,
            fb_pc=fb_pc, fws_pc=fws_pc, fis_bc=fis_bc,
            ident=identm,
        ))
    res = run_bass_kernel_spmd(nc, in_maps, core_ids=list(range(8)))
    outs = [res.results[c]["y"] for c in range(8)]
    full = np.concatenate(outs, axis=0).reshape(2, T, D)
    return full.astype(np.float32)
